# version 02
# baseline (speedup 1.0000x reference)
import math
import contextlib
import numpy as np
import ml_dtypes
import concourse.bass as bass
import concourse.mybir as mybir
from concourse.bass_utils import run_bass_kernel_spmd

F32 = mybir.dt.float32
BF16 = mybir.dt.bfloat16
I32 = mybir.dt.int32
AF = mybir.ActivationFunctionType
ALU = mybir.AluOpType
AX = mybir.AxisListType.X

NCORES = 8
N = 8192
DM = 4096
TLOC = N // NCORES
NT = TLOC // 128
NDT = DM // 128
NCH = 16

BF = ml_dtypes.bfloat16

NF4 = [-1.0, -0.6961928009986877, -0.5250730514526367, -0.39491748809814453,
       -0.28444138169288635, -0.18477343022823334, -0.09105003625154495, 0.0,
       0.07958029955625534, 0.16093020141124725, 0.24611230194568634,
       0.33791524171829224, 0.44070982933044434, 0.5626170039176941,
       0.7229568362236023, 1.0]

H1, HKV, HD = 32, 8, 128
_s0 = 2 ** (-2 ** (-(math.log2(H1) - 3)))
SLOPES = [_s0 * _s0 ** i for i in range(H1)]
WIN = [max(640, int(140.0 / s) + 1) for s in SLOPES]
WC = [int(math.ceil(w / 512.0)) + 1 for w in WIN]
B_SHIFT = 40.0

SKV_A = [min(max(WC[4 * kv + g] for g in range(4)), 8) for kv in range(HKV)]
SKV_B = [min(max(WC[4 * kv + g] for g in range(4)), 16) for kv in range(HKV)]

H0, E0 = 8, 64

HF = 11008
HSH = [1408] * 6 + [1280] * 2
HOFF = [sum(HSH[:c]) for c in range(NCORES)]



class _Ct:
    __slots__ = ("v",)

    def __init__(self):
        self.v = 0


class Sch:
    ENGINES = ("sync", "gpsimd", "tensor", "vector", "scalar")

    def __init__(self, nc, ctx):
        self.nc = nc
        self.ctx = ctx
        self.q = {e: [] for e in self.ENGINES}
        self.sems = {}

    def sem(self, name):
        if name not in self.sems:
            h = self.ctx.enter_context(self.nc.semaphore(name))
            self.sems[name] = (h, _Ct())
        return self.sems[name]

    def val(self, name):
        return self.sem(name)[1].v

    def wait(self, eng, name, v=None):
        h, ct = self.sem(name)
        if v is None:
            v = ct.v
        if v <= 0:
            return
        self.q[eng].append(lambda e, h=h, v=v: e.wait_ge(h, v))

    def op(self, eng, build, inc=None, n=1):
        if inc is not None:
            h, ct = self.sem(inc)
            ct.v += n

            def f(e, build=build, h=h, n=n):
                build(e).then_inc(h, n)
            self.q[eng].append(f)
            return ct.v
        self.q[eng].append(build)
        return None

    def dma(self, eng, out, in_, inc=None):
        return self.op(eng, lambda e, o=out, i=in_: e.dma_start(out=o, in_=i),
                       inc=inc, n=16)

    def mm_group(self, pairs, psum_ap, inc):
        nmm = len(pairs)
        v = None
        for i, (lh, rh) in enumerate(pairs):
            v = self.op('tensor',
                        lambda e, lh=lh, rh=rh, s=(i == 0), st=(i == nmm - 1):
                        e.matmul(psum_ap, lh, rh, start=s, stop=st,
                                 skip_group_check=True),
                        inc=(inc if i == nmm - 1 else None))
        return v

    def run(self, block):
        for eng in self.ENGINES:
            ops = self.q[eng]

            def body(handle, ops=ops):
                for fn in ops:
                    fn(handle)
            getattr(block, eng)(body)



def tile_w(w_t):
    K, O = w_t.shape
    nd, no = K // 128, O // 128
    return np.ascontiguousarray(
        w_t.reshape(nd, 128, no, 128).transpose(2, 1, 0, 3).reshape(no, 128, nd * 128))


def cols128(b):
    no = b.shape[0] // 128
    return np.ascontiguousarray(b.reshape(no, 128).T.astype(np.float32))


def _bfc(a):
    return np.ascontiguousarray(np.asarray(a).astype(BF))


def _f32c(a):
    return np.ascontiguousarray(np.asarray(a).astype(np.float32))


def _rel_consts():
    i = np.arange(128)[:, None]
    j = np.arange(512)[None, :]
    rel = (j - i).astype(np.float32)
    out = [rel]
    for m in range(4):
        r = rel.copy()
        r[(-m * 128 + j - i) > 0] = -1e33
        out.append(r)
    return np.ascontiguousarray(np.stack(out))



def emit_ln_tile(sch, x_sb, xn_sb, scr, stt, eps=1e-5):
    v = sch.op('vector', lambda e: e.reduce_sum(stt[:, 0:1], x_sb[:], axis=AX), inc='ve')
    a = sch.op('scalar', lambda e: e.activation(scr[:], x_sb[:], AF.Square,
                                                accum_out=stt[:, 1:2]), inc='ac')
    sch.wait('vector', 'ac', a)
    sch.wait('vector', 've', v)
    v = sch.op('vector', lambda e: e.tensor_scalar_mul(stt[:, 2:3], stt[:, 0:1], 1.0 / DM), inc='ve')
    sch.wait('vector', 've', v)
    v = sch.op('vector', lambda e: e.tensor_mul(stt[:, 4:5], stt[:, 2:3], stt[:, 2:3]), inc='ve')
    v = sch.op('vector', lambda e: e.tensor_scalar(stt[:, 3:4], stt[:, 1:2], 1.0 / DM,
                                                   None, ALU.mult), inc='ve')
    sch.wait('vector', 've', v)
    v = sch.op('vector', lambda e: e.tensor_sub(stt[:, 5:6], stt[:, 3:4], stt[:, 4:5]), inc='ve')
    sch.wait('vector', 've', v)
    v = sch.op('vector', lambda e: e.tensor_scalar_add(stt[:, 5:6], stt[:, 5:6], float(eps)), inc='ve')
    sch.wait('vector', 've', v)
    v = sch.op('vector', lambda e: e.reciprocal(stt[:, 6:7], stt[:, 5:6]), inc='ve')
    sch.wait('scalar', 've', v)
    a = sch.op('scalar', lambda e: e.activation(stt[:, 7:8], stt[:, 6:7], AF.Sqrt), inc='ac')
    sch.wait('vector', 'ac', a)
    v = sch.op('vector', lambda e: e.tensor_scalar(xn_sb[:], x_sb[:], stt[:, 2:3],
                                                   stt[:, 7:8], ALU.subtract, ALU.mult), inc='ve')
    return v



def build_ln0():
    nc = bass.Bass("TRN2", target_bir_lowering=False)
    x = nc.dram_tensor("x", [TLOC, DM], F32, kind="ExternalInput").ap()
    xn = nc.dram_tensor("xn", [TLOC, DM], BF16, kind="ExternalOutput").ap()
    with contextlib.ExitStack() as ctx:
        sch = Sch(nc, ctx)
        xsb = [ctx.enter_context(nc.sbuf_tensor(f"x{i}", [128, DM], F32)) for i in range(2)]
        xnb = [ctx.enter_context(nc.sbuf_tensor(f"xn{i}", [128, DM], BF16)) for i in range(2)]
        scr = ctx.enter_context(nc.sbuf_tensor("scr", [128, DM], F32))
        stats = ctx.enter_context(nc.sbuf_tensor("st", [128, 8 * NT], F32))
        block = ctx.enter_context(nc.Block())

        fin = [None] * NT
        st_v = 0
        for t in range(NT):
            b = t % 2
            if t >= 2:
                vv, av, sv = fin[t - 2]
                sch.wait('gpsimd', 've', vv)
                sch.wait('gpsimd', 'ac', av)
                sch.wait('gpsimd', 'st%d' % b, sv)
            ld = sch.dma('gpsimd', xsb[b][:], x[t * 128:(t + 1) * 128, :], inc='ld%d' % b)
            sch.wait('vector', 'ld%d' % b, ld)
            sch.wait('scalar', 'ld%d' % b, ld)
            v = emit_ln_tile(sch, xsb[b], xnb[b], scr, stats[:, 8 * t:8 * (t + 1)])
            sch.wait('sync', 've', v)
            st_v = sch.dma('sync', xn[t * 128:(t + 1) * 128, :], xnb[b][:], inc='st%d' % b)
            fin[t] = (v, sch.val('ac'), st_v)
        sch.wait('sync', 'st0', sch.val('st0'))
        sch.wait('sync', 'st1', sch.val('st1'))
        sch.run(block)
    return nc



def build_l0a():
    nc = bass.Bass("TRN2", target_bir_lowering=False)
    NQKV = 12
    xnT = nc.dram_tensor("xnT", [DM, TLOC], BF16, kind="ExternalInput").ap()
    wqkv = nc.dram_tensor("wqkv", [NQKV, 128, DM], F32, kind="ExternalInput").ap()
    bqkv = nc.dram_tensor("bqkv", [128, NQKV], F32, kind="ExternalInput").ap()
    wrout = nc.dram_tensor("wrout", [128, NDT * 8], F32, kind="ExternalInput").ap()
    wa = nc.dram_tensor("wa", [1, 128, DM], F32, kind="ExternalInput").ap()
    wbase = nc.dram_tensor("wbase", [NDT, 128, DM], F32, kind="ExternalInput").ap()
    bbase = nc.dram_tensor("bbase", [128, NDT], F32, kind="ExternalInput").ap()
    bstk = nc.dram_tensor("bstk", [NDT, 128, 128], F32, kind="ExternalInput").ap()
    sel8 = nc.dram_tensor("sel8", [8, 128], F32, kind="ExternalInput").ap()
    idf = nc.dram_tensor("idf", [128, 128], F32, kind="ExternalInput").ap()
    idb = nc.dram_tensor("idb", [128, 128], BF16, kind="ExternalInput").ap()

    qkv_o = nc.dram_tensor("qkv", [NQKV * 128, TLOC], BF16, kind="ExternalOutput").ap()
    g0_o = nc.dram_tensor("g0", [DM, TLOC], F32, kind="ExternalOutput").ap()

    with contextlib.ExitStack() as ctx:
        sch = Sch(nc, ctx)
        S = lambda name, shape, dt=F32: ctx.enter_context(nc.sbuf_tensor(name, shape, dt))
        xn = S("xn", [128, NDT, TLOC], BF16)
        wsl = [S(f"wsl{i}", [128, DM], BF16) for i in range(2)]
        wrs = S("wrs", [128, NDT * 8], BF16)
        idf_s = S("idf_s", [128, 128], F32)
        idb_s = S("idb_s", [128, 128], BF16)
        sel_s = S("sel_s", [8, 128], BF16)
        bq_s = S("bq_s", [128, NQKV], F32)
        bb_s = S("bb_s", [128, NDT], F32)
        bst_s = S("bst_s", [128, NDT, 128], BF16)
        lg = S("lg", [8, TLOC], F32)
        lgt = S("lgt", [128, NT, 8], F32)
        cstat = S("cstat", [128, NT, 8], F32)
        cmsk = S("cmsk", [128, 8], F32)
        cl2 = S("cl2", [128, 8], F32)
        cel = S("cel", [128, 8], F32)
        ctm = S("ctm", [128, NT, 8], BF16)
        cfm = S("cfm", [8, TLOC], BF16)
        crep = S("crep", [128, TLOC], F32)
        hw = S("hw", [128, TLOC], BF16)
        ev = [S(f"ev{i}", [128, TLOC], BF16) for i in range(2)]
        gv = [S(f"gv{i}", [128, TLOC], F32) for i in range(2)]

        ps = [ctx.enter_context(nc.psum_tensor(f"ps{i}", [128, 512], F32)) for i in range(2)]
        psr = ctx.enter_context(nc.psum_tensor("psr", [8, 512], F32))
        pstf = ctx.enter_context(nc.psum_tensor("pstf", [128, 512], F32))
        pstb = ctx.enter_context(nc.psum_tensor("pstb", [128, 1024], BF16))
        block = ctx.enter_context(nc.Block())

        ld = sch.dma('gpsimd', xn[:], xnT.rearrange("(nd p) t -> p nd t", p=128), inc='ld')
        ld = sch.dma('gpsimd', wrs[:], wrout[:], inc='ld')
        ld = sch.dma('gpsimd', idf_s[:], idf[:], inc='ld')
        ld = sch.dma('gpsimd', idb_s[:], idb[:], inc='ld')
        ld = sch.dma('gpsimd', sel_s[:], sel8[:], inc='ld')
        ld = sch.dma('gpsimd', bq_s[:], bqkv[:], inc='ld')
        ld = sch.dma('gpsimd', bb_s[:], bbase[:], inc='ld')
        ld = sch.dma('gpsimd', bst_s[:], bstk.transpose([1, 0, 2]), inc='ld')
        LD0 = ld

        sch.wait('tensor', 'ld', LD0)
        for c in range(2):
            if c == 1:
                sch.wait('tensor', 've', vv)
            pairs = [(wrs[:, d * 8:(d + 1) * 8], xn[:, d, c * 512:(c + 1) * 512])
                     for d in range(NDT)]
            pv = sch.mm_group(pairs, psr[:], inc='pe')
            sch.wait('vector', 'pe', pv)
            vv = sch.op('vector', lambda e, c=c: e.tensor_copy(lg[:, c * 512:(c + 1) * 512], psr[:]), inc='ve')
        sch.wait('tensor', 've', vv)
        for t in range(NT):
            tv = sch.op('tensor', lambda e, t=t: e.transpose(
                pstf[:, 0:8], lg[0:8, t * 128:(t + 1) * 128], idf_s[0:8, 0:8]), inc='pe')
            sch.wait('vector', 'pe', tv)
            vv = sch.op('vector', lambda e, t=t: e.tensor_copy(lgt[:, t, :], pstf[:, 0:8]), inc='ve')
            sch.wait('tensor', 've', vv)
        for t in range(NT):
            st = cstat[:, t, :]
            l_ = lgt[:, t, :]
            v = sch.op('vector', lambda e, l_=l_, st=st: e.reduce_max(st[:, 0:1], l_, axis=AX), inc='ve')
            sch.wait('vector', 've', v)
            v = sch.op('vector', lambda e, l_=l_, st=st: e.tensor_scalar(
                cmsk[:], l_, st[:, 0:1], -1e30, ALU.is_equal, ALU.mult), inc='ve')
            sch.wait('vector', 've', v)
            v = sch.op('vector', lambda e, l_=l_: e.tensor_add(cl2[:], l_, cmsk[:]), inc='ve')
            sch.wait('vector', 've', v)
            v = sch.op('vector', lambda e, st=st: e.reduce_max(st[:, 1:2], cl2[:], axis=AX), inc='ve')
            sch.wait('vector', 've', v)
            v = sch.op('vector', lambda e, st=st: e.tensor_scalar_mul(st[:, 2:3], st[:, 0:1], -1.0), inc='ve')
            sch.wait('scalar', 've', v)
            a = sch.op('scalar', lambda e, l_=l_, st=st: e.activation(
                cel[:], l_, AF.Exp, bias=st[:, 2:3]), inc='ac')
            sch.wait('vector', 'ac', a)
            v = sch.op('vector', lambda e, st=st: e.tensor_sub(st[:, 3:4], st[:, 1:2], st[:, 0:1]), inc='ve')
            sch.wait('scalar', 've', v)
            a = sch.op('scalar', lambda e, st=st: e.activation(st[:, 4:5], st[:, 3:4], AF.Exp), inc='ac')
            sch.wait('vector', 'ac', a)
            v = sch.op('vector', lambda e, st=st: e.tensor_scalar_add(st[:, 5:6], st[:, 4:5], 1.0), inc='ve')
            sch.wait('vector', 've', v)
            v = sch.op('vector', lambda e, st=st: e.reciprocal(st[:, 6:7], st[:, 5:6]), inc='ve')
            sch.wait('vector', 've', v)
            v = sch.op('vector', lambda e, st=st: e.tensor_scalar_mul(st[:, 7:8], st[:, 6:7], 2.0), inc='ve')
            v = sch.op('vector', lambda e, l_=l_, st=st: e.tensor_scalar(
                cmsk[:], l_, st[:, 1:2], None, ALU.is_ge), inc='ve')
            sch.wait('vector', 've', v)
            v = sch.op('vector', lambda e: e.tensor_mul(cl2[:], cel[:], cmsk[:]), inc='ve')
            sch.wait('vector', 've', v)
            v = sch.op('vector', lambda e, t=t, st=st: e.tensor_scalar(
                ctm[:, t, :], cl2[:], st[:, 7:8], None, ALU.mult), inc='ve')
            sch.wait('tensor', 've', v)
            tv = sch.op('tensor', lambda e, t=t: e.transpose(
                pstb[0:8, 0:128], ctm[:, t, :], idb_s[:]), inc='pe')
            sch.wait('vector', 'pe', tv)
            v = sch.op('vector', lambda e, t=t: e.tensor_copy(
                cfm[0:8, t * 128:(t + 1) * 128], pstb[0:8, 0:128]), inc='ve')
            sch.wait('tensor', 've', v)
        for c in range(2):
            pv = sch.mm_group([(sel_s[:], cfm[0:8, c * 512:(c + 1) * 512])], ps[c][:], inc='pe')
            sch.wait('vector', 'pe', pv)
            vv = sch.op('vector', lambda e, c=c: e.tensor_copy(crep[:, c * 512:(c + 1) * 512], ps[c][:]), inc='ve')
        nld = {0: 0, 1: 0}
        nld[0] += 1
        wa_v = sch.dma('gpsimd', wsl[0][:], wa[0, :, :], inc='ldw0')
        sch.wait('tensor', 'ldw0', wa_v)
        sch.wait('tensor', 've', vv)
        for c in range(2):
            pairs = [(wsl[0][:, d * 128:(d + 1) * 128], xn[:, d, c * 512:(c + 1) * 512])
                     for d in range(NDT)]
            pv = sch.mm_group(pairs, ps[c][:], inc='pe')
            sch.wait('vector', 'pe', pv)
            vv = sch.op('vector', lambda e, c=c: e.tensor_mul(
                hw[:, c * 512:(c + 1) * 512], ps[c][:], crep[:, c * 512:(c + 1) * 512]), inc='ve')
        HW_DONE = vv

        def proj(wdram, n_o, epilogue, tag, extra=None, first_wait=None,
                 pre_guard=None):
            ev = 'evac_' + tag
            for o in range(n_o):
                b = o % 2
                if o >= 2:
                    sch.wait('gpsimd', ev, 2 * (o - 1))
                elif pre_guard:
                    for pg in pre_guard:
                        sch.wait('gpsimd', pg[0], pg[1])
                nld[b] += 1
                sch.dma('gpsimd', wsl[b][:], wdram[o, :, :], inc='ldw%d' % b)
                sch.wait('tensor', 'ldw%d' % b, 16 * nld[b])
                if first_wait is not None:
                    for fw in first_wait:
                        sch.wait('tensor', fw[0], fw[1])
                for c in range(2):
                    pairs = [(wsl[b][:, d * 128:(d + 1) * 128],
                              xn[:, d, c * 512:(c + 1) * 512]) for d in range(NDT)]
                    if extra is not None:
                        pairs += extra(o, c)
                    if 2 * o + c - 1 >= 1:
                        sch.wait('tensor', ev, 2 * o + c - 1)
                    pv = sch.mm_group(pairs, ps[c][:], inc='pe')
                    epilogue(o, c, ps[c], pv, ev)

        st_of = {}

        def ep_qkv(o, c, psum, pv, evs):
            b2 = o % 2
            sch.wait('vector', 'pe', pv)
            if c == 0 and (o - 2) in st_of:
                sch.wait('vector', 'st', sch.val('st'))
            v = sch.op('vector', lambda e, o=o, c=c, psum=psum, b2=b2: e.tensor_scalar(
                ev[b2][:, c * 512:(c + 1) * 512], psum[:], bq_s[:, o:o + 1], None, ALU.add),
                inc=evs)
            if c == 1:
                sch.wait('sync', evs, v)
                st_of[o] = sch.dma('sync', qkv_o[o * 128:(o + 1) * 128, :], ev[b2][:], inc='st')

        proj(wqkv, NQKV, ep_qkv, 'qkv', first_wait=[('ve', HW_DONE)],
             pre_guard=[('pe', sch.val('pe'))])

        def extra_base(o, c):
            return [(bst_s[:, o, :], hw[:, c * 512:(c + 1) * 512])]

        stg_of = {}

        def ep_base(o, c, psum, pv, evs):
            b2 = o % 2
            sch.wait('scalar', 'pe', pv)
            if c == 0 and (o - 2) in stg_of:
                sch.wait('scalar', 'st', sch.val('st'))
            v = sch.op('scalar', lambda e, o=o, c=c, psum=psum, b2=b2: e.activation(
                gv[b2][:, c * 512:(c + 1) * 512], psum[:], AF.Silu,
                bias=bb_s[:, o:o + 1]), inc=evs)
            if c == 1:
                sch.wait('sync', evs, v)
                stg_of[o] = sch.dma('sync', g0_o[o * 128:(o + 1) * 128, :], gv[b2][:], inc='st')

        proj(wbase, NDT, ep_base, 'base', extra=extra_base,
             first_wait=[('ve', HW_DONE), ('evac_qkv', 24)],
             pre_guard=[('evac_qkv', 24)])

        sch.wait('sync', 'st', sch.val('st'))
        sch.run(block)
    return nc



def build_l0b():
    nc = bass.Bass("TRN2", target_bir_lowering=False)
    x = nc.dram_tensor("x", [TLOC, DM], F32, kind="ExternalInput").ap()
    qf = nc.dram_tensor("qf", [512, TLOC], BF16, kind="ExternalInput").ap()
    kf = nc.dram_tensor("kf", [512, N], BF16, kind="ExternalInput").ap()
    vt = nc.dram_tensor("vt", [N, 512], BF16, kind="ExternalInput").ap()
    g0t = nc.dram_tensor("g0t", [TLOC, DM], F32, kind="ExternalInput").ap()
    wo = nc.dram_tensor("wo", [512, DM], F32, kind="ExternalInput").ap()
    bo = nc.dram_tensor("bo", [1, DM], F32, kind="ExternalInput").ap()
    ones1 = nc.dram_tensor("ones1", [1, 128], F32, kind="ExternalInput").ap()
    idb = nc.dram_tensor("idb", [128, 128], BF16, kind="ExternalInput").ap()

    x1_o = nc.dram_tensor("x1", [TLOC, DM], F32, kind="ExternalOutput").ap()
    xn1_o = nc.dram_tensor("xn1", [TLOC, DM], BF16, kind="ExternalOutput").ap()

    NK = N // 512
    NKT = N // 128
    with contextlib.ExitStack() as ctx:
        sch = Sch(nc, ctx)
        S = lambda name, shape, dt=F32: ctx.enter_context(nc.sbuf_tensor(name, shape, dt))
        o_sb = S("o_sb", [128, 4, TLOC], BF16)
        idb_s = S("idb_s", [128, 128], BF16)
        ones_s = S("ones_s", [1, 128], BF16)
        bo_s = S("bo_s", [1, DM], BF16)

        pss = [ctx.enter_context(nc.psum_tensor(f"pss{i}", [128, 512], F32)) for i in range(2)]
        pstb = [ctx.enter_context(nc.psum_tensor(f"pstb{i}", [128, 1024], BF16)) for i in range(2)]
        pso = ctx.enter_context(nc.psum_tensor("pso", [64, 512], F32))
        psp = [ctx.enter_context(nc.psum_tensor(f"psp{i}", [128, 512], F32)) for i in range(2)]
        block = ctx.enter_context(nc.Block())

        ld0 = sch.dma('gpsimd', idb_s[:], idb[:], inc='ld')
        ld0 = sch.dma('gpsimd', ones_s[:], ones1[:], inc='ld')
        ld0 = sch.dma('gpsimd', bo_s[:], bo[:], inc='ld')

        with contextlib.ExitStack() as actx:
            S2 = lambda name, shape, dt: actx.enter_context(nc.sbuf_tensor(name, shape, dt))
            q_sb = S2("q_sb", [128, 4, TLOC], BF16)
            kp = S2("kp", [128, N], BF16)
            vp = S2("vp", [128, NKT, 128], BF16)
            att = S2("att", [128, N], BF16)
            attn = S2("attn", [128, N], BF16)
            panels = S2("panels", [128, NKT, 512], BF16)
            mcols = S2("mcols", [128, NK + 8], F32)
            zcols = S2("zcols", [128, NK + 8], F32)

            ld0 = sch.dma('gpsimd', q_sb[:], qf.rearrange("(pp p) t -> p pp t", p=128), inc='ld')
            sch.wait('tensor', 'ld', ld0)
            kv_users_ac = 0
            prev_exp = 0
            for p in range(4):
                if p > 0:
                    sch.wait('gpsimd', 'ac', kv_users_ac)
                    sch.wait('gpsimd', 'pe', sch.val('pe'))
                lk = sch.dma('gpsimd', kp[:], kf[p * 128:(p + 1) * 128, :], inc='ldk')
                lv = sch.dma('gpsimd', vp[:],
                             vt[:, p * 128:(p + 1) * 128]
                             .rearrange("(kt q) d -> q kt d", q=128), inc='ldk')
                sch.wait('tensor', 'ldk', lv)
                for hh in range(2):
                    qh = q_sb[hh * 64:(hh + 1) * 64, :, :]
                    kh = kp[hh * 64:(hh + 1) * 64, :]
                    for qtg in range(2):
                        pan_v = None
                        for qt in range(4):
                            tqi = qtg * 4 + qt
                            lhs = qh[:, p, tqi * 128:(tqi + 1) * 128]
                            rmax_v = [0] * NK
                            for c in range(NK):
                                b = c % 2
                                if c >= 2:
                                    sch.wait('tensor', 've', rmax_v[c - 2])
                                else:
                                    sch.wait('tensor', 'ac', prev_exp)
                                pv = sch.mm_group([(lhs, kh[:, c * 512:(c + 1) * 512])],
                                                  pss[b][:], inc='pe')
                                sch.wait('vector', 'pe', pv)
                                rmax_v[c] = sch.op('vector', lambda e, c=c, b=b: e.reduce_max(
                                    mcols[:, c:c + 1], pss[b][:], axis=AX), inc='ve')
                            sch.wait('vector', 've', rmax_v[NK - 1])
                            v = sch.op('vector', lambda e: e.reduce_max(
                                mcols[:, NK:NK + 1], mcols[:, 0:NK], axis=AX), inc='ve')
                            sch.wait('vector', 've', v)
                            v = sch.op('vector', lambda e: e.tensor_scalar_mul(
                                mcols[:, NK + 1:NK + 2], mcols[:, NK:NK + 1], -1.0), inc='ve')
                            sch.wait('scalar', 've', v)
                            exp_v = [0] * NK
                            for c in range(NK):
                                b = c % 2
                                sch.wait('tensor', 've', v)
                                if c >= 2:
                                    sch.wait('tensor', 'ac', exp_v[c - 2])
                                pv = sch.mm_group([(lhs, kh[:, c * 512:(c + 1) * 512])],
                                                  pss[b][:], inc='pe')
                                sch.wait('scalar', 'pe', pv)
                                exp_v[c] = sch.op('scalar', lambda e, c=c, b=b: e.activation(
                                    att[:, c * 512:(c + 1) * 512], pss[b][:], AF.Exp,
                                    bias=mcols[:, NK + 1:NK + 2],
                                    accum_out=zcols[:, c:c + 1]), inc='ac')
                            prev_exp = exp_v[NK - 1]
                            sch.wait('vector', 'ac', exp_v[NK - 1])
                            v = sch.op('vector', lambda e: e.reduce_sum(
                                zcols[:, NK:NK + 1], zcols[:, 0:NK], axis=AX), inc='ve')
                            sch.wait('vector', 've', v)
                            v = sch.op('vector', lambda e: e.reciprocal(
                                zcols[:, NK + 1:NK + 2], zcols[:, NK:NK + 1]), inc='ve')
                            sch.wait('vector', 've', v)
                            v = sch.op('vector', lambda e: e.tensor_scalar_mul(
                                zcols[:, NK + 2:NK + 3], zcols[:, NK + 1:NK + 2],
                                1.0 / math.sqrt(512.0)), inc='ve')
                            sch.wait('vector', 've', v)
                            normv = sch.op('vector', lambda e: e.tensor_scalar(
                                attn[:], att[:], zcols[:, NK + 2:NK + 3], None, ALU.mult), inc='ve')
                            sch.wait('tensor', 've', normv)
                            for kt in range(NKT):
                                half = (kt // 4) % 2
                                if kt >= 8:
                                    sch.wait('tensor', 've', pan_cp[kt - 8])
                                tv = sch.op('tensor', lambda e, kt=kt, half=half: e.transpose(
                                    pstb[half][:, (kt % 4) * 128:(kt % 4) * 128 + 128],
                                    attn[:, kt * 128:(kt + 1) * 128], idb_s[:]), inc='pe')
                                if kt % 4 == 3:
                                    sch.wait('vector', 'pe', tv)
                                    pv4 = sch.op('vector', lambda e, kt=kt, half=half, qt=qt: e.tensor_copy(
                                        panels[:, kt - 3:kt + 1, qt * 128:(qt + 1) * 128],
                                        pstb[half][:, 0:512]
                                        .rearrange("p (a b) -> p a b", a=4)), inc='ve')
                                    if kt == 3:
                                        pan_cp = [0] * NKT
                                    for z_ in range(kt - 3, kt + 1):
                                        pan_cp[z_] = pv4
                            pan_v = pan_cp[NKT - 1]
                        sch.wait('tensor', 've', pan_v)
                        pairs = [(vp[:, kt, hh * 64:(hh + 1) * 64], panels[:, kt, :])
                                 for kt in range(NKT)]
                        pvv = sch.mm_group(pairs, pso[:], inc='pe')
                        sch.wait('vector', 'pe', pvv)
                        ov = sch.op('vector', lambda e, hh=hh, p=p, qtg=qtg: e.tensor_copy(
                            o_sb[hh * 64:(hh + 1) * 64, p, qtg * 512:(qtg + 1) * 512],
                            pso[:]), inc='ve')
                        sch.wait('tensor', 've', ov)
                        kv_users_ac = sch.val('ac')
            O_DONE = sch.val('ve')

        sch.wait('gpsimd', 've', O_DONE)
        sch.wait('gpsimd', 'ac', sch.val('ac'))
        sch.wait('gpsimd', 'pe', sch.val('pe'))
        with contextlib.ExitStack() as pctx:
            S3 = lambda name, shape, dt: pctx.enter_context(nc.sbuf_tensor(name, shape, dt))
            wo_sb = S3("wo_sb", [128, 4, DM], BF16)
            xt = [S3(f"xt{i}", [128, 512], F32) for i in range(2)]
            gt = [S3(f"gt{i}", [128, 512], F32) for i in range(2)]
            x1b = [S3(f"x1b{i}", [128, DM], F32) for i in range(2)]
            xnb = [S3(f"xnb{i}", [128, DM], BF16) for i in range(2)]
            tmp = [S3(f"tmp{i}", [128, 512], F32) for i in range(2)]
            scr = S3("scr", [128, DM], F32)
            stats = S3("stats", [128, 8 * 2], F32)

            lw = sch.dma('gpsimd', wo_sb[:], wo.rearrange("(j p) o -> p j o", p=128), inc='ld')
            sch.wait('tensor', 'ld', lw)
            NOC = DM // 512
            ldxg_v = {}
            fin = {}
            for t in range(NT):
                tb = t % 2
                if t >= 2:
                    vv, av, sv = fin[t - 2]
                    sch.wait('gpsimd', 've', vv)
                    sch.wait('gpsimd', 'ac', av)
                    sch.wait('gpsimd', 'st%d' % tb, sv)
                for oc in range(NOC):
                    u = t * NOC + oc
                    b = u % 2
                    if u >= 2:
                        sch.wait('gpsimd', 've', ldxg_v[u - 2])
                    lx = sch.dma('gpsimd', xt[b][:],
                                 x[t * 128:(t + 1) * 128, oc * 512:(oc + 1) * 512], inc='ldx%d' % b)
                    lx = sch.dma('gpsimd', gt[b][:],
                                 g0t[t * 128:(t + 1) * 128, oc * 512:(oc + 1) * 512], inc='ldx%d' % b)
                    pairs = [(o_sb[:, j, t * 128:(t + 1) * 128],
                              wo_sb[:, j, oc * 512:(oc + 1) * 512]) for j in range(4)]
                    pairs.append((ones_s[:], bo_s[0:1, oc * 512:(oc + 1) * 512]))
                    if u >= 2:
                        sch.wait('tensor', 've', ldxg_v[u - 2])
                    pv = sch.mm_group(pairs, psp[b][:], inc='pe')
                    sch.wait('vector', 'pe', pv)
                    sch.wait('vector', 'ldx%d' % b, lx)
                    v = sch.op('vector', lambda e, b=b: e.tensor_mul(tmp[b][:], psp[b][:], gt[b][:]), inc='ve')
                    sch.wait('vector', 've', v)
                    v = sch.op('vector', lambda e, b=b, t=t, oc=oc, tb=tb: e.tensor_add(
                        x1b[tb][:, oc * 512:(oc + 1) * 512], tmp[b][:], xt[b][:]), inc='ve')
                    ldxg_v[u] = v
                sch.wait('vector', 've', v)
                sch.wait('scalar', 've', v)
                lnv = emit_ln_tile(sch, x1b[tb], xnb[tb], scr, stats[:, 8 * tb:8 * (tb + 1)])
                sch.wait('sync', 've', lnv)
                sv = sch.dma('sync', x1_o[t * 128:(t + 1) * 128, :], x1b[tb][:], inc='st%d' % tb)
                sv = sch.dma('sync', xn1_o[t * 128:(t + 1) * 128, :], xnb[tb][:], inc='st%d' % tb)
                fin[t] = (lnv, sch.val('ac'), sv)
            sch.wait('sync', 'st0', sch.val('st0'))
            sch.wait('sync', 'st1', sch.val('st1'))
        sch.run(block)
    return nc



HSH_PAD = 1408
NQKV1 = 48


def build_l1a():
    nc = bass.Bass("TRN2", target_bir_lowering=False)
    xnT = nc.dram_tensor("xnT", [DM, TLOC], BF16, kind="ExternalInput").ap()
    wqkv = nc.dram_tensor("wqkv", [NQKV1, 128, DM], F32, kind="ExternalInput").ap()
    idb = nc.dram_tensor("idb", [128, 128], BF16, kind="ExternalInput").ap()
    cod_u = nc.dram_tensor("cod_u", [DM, HSH_PAD], I32, kind="ExternalInput").ap()
    amx_u = nc.dram_tensor("amx_u", [64, HSH_PAD], F32, kind="ExternalInput").ap()
    cod_d = nc.dram_tensor("cod_d", [HSH_PAD, DM], I32, kind="ExternalInput").ap()
    amx_d = nc.dram_tensor("amx_d", [22, DM], F32, kind="ExternalInput").ap()
    la_up = nc.dram_tensor("la_up", [128, NDT * 16], F32, kind="ExternalInput").ap()

    qkvt_o = nc.dram_tensor("qkvt", [NQKV1, 2, 128, 4, 128], BF16, kind="ExternalOutput").ap()
    wup_o = nc.dram_tensor("wup_sh", [DM, HSH_PAD], BF16, kind="ExternalOutput").ap()
    wdn_o = nc.dram_tensor("wdn_sh", [HSH_PAD, DM], BF16, kind="ExternalOutput").ap()
    u1_o = nc.dram_tensor("u1", [16, TLOC], BF16, kind="ExternalOutput").ap()

    with contextlib.ExitStack() as ctx:
        sch = Sch(nc, ctx)
        S = lambda name, shape, dt=F32: ctx.enter_context(nc.sbuf_tensor(name, shape, dt))
        xn = S("xn", [128, NDT, TLOC], BF16)
        wsl = [S(f"wsl{i}", [128, DM], BF16) for i in range(2)]
        idb_s = S("idb_s", [128, 128], BF16)
        la_s = S("la_s", [128, NDT * 16], BF16)
        evq = [S(f"evq{i}", [128, 512], BF16) for i in range(2)]
        qn = [S(f"qn{i}", [128, 512], BF16) for i in range(2)]
        ms = S("ms", [128, 16], F32)
        u1s = S("u1s", [16, TLOC], BF16)
        sqd = S("sqd", [128, 512], BF16)
        cod = [S(f"cod{i}", [128, DM], I32) for i in range(2)]
        amx = [S(f"amx{i}", [128, DM], F32) for i in range(2)]
        dacc = [S(f"dacc{i}", [128, DM], BF16) for i in range(2)]
        dtmp = S("dtmp", [128, DM], BF16)

        ps = [ctx.enter_context(nc.psum_tensor(f"ps{i}", [128, 512], F32)) for i in range(2)]
        psu = ctx.enter_context(nc.psum_tensor("psu", [16, 512], F32))
        pstb = [ctx.enter_context(nc.psum_tensor(f"pstb{i}", [128, 1024], BF16)) for i in range(2)]
        block = ctx.enter_context(nc.Block())

        ld0 = sch.dma('gpsimd', xn[:], xnT.rearrange("(nd p) t -> p nd t", p=128), inc='ld')
        ld0 = sch.dma('gpsimd', idb_s[:], idb[:], inc='ld')
        ld0 = sch.dma('gpsimd', la_s[:], la_up[:], inc='ld')
        sch.wait('tensor', 'ld', ld0)

        for c in range(2):
            if c == 1:
                sch.wait('tensor', 've', uv)
            pairs = [(la_s[:, d * 16:(d + 1) * 16], xn[:, d, c * 512:(c + 1) * 512])
                     for d in range(NDT)]
            pv = sch.mm_group(pairs, psu[:], inc='pe')
            sch.wait('vector', 'pe', pv)
            uv = sch.op('vector', lambda e, c=c: e.tensor_copy(
                u1s[0:16, c * 512:(c + 1) * 512], psu[:]), inc='ve')
        sch.wait('sync', 've', uv)
        sch.dma('sync', u1_o[:], u1s[:], inc='st')

        nld = {0: 0, 1: 0}
        EPS1 = 1e-6
        bv_of = {}
        av_prev = {}
        vv_prev = {}
        st_q = {}
        for o in range(NQKV1):
            b = o % 2
            if o >= 2:
                sch.wait('gpsimd', 've', bv_of[(o - 2, 1)])
            nld[b] += 1
            sch.dma('gpsimd', wsl[b][:], wqkv[o, :, :], inc='ldw%d' % b)
            sch.wait('tensor', 'ldw%d' % b, 16 * nld[b])
            for c in range(2):
                u = 2 * o + c
                pairs = [(wsl[b][:, d * 128:(d + 1) * 128],
                          xn[:, d, c * 512:(c + 1) * 512]) for d in range(NDT)]
                if u >= 2:
                    sch.wait('tensor', 've', bv_of[(o - 1, c)])
                pv = sch.mm_group(pairs, ps[c][:], inc='pe')
                sch.wait('vector', 'pe', pv)
                if u >= 2:
                    sch.wait('vector', 'st%d' % (u % 2), st_q.get((o - 1, c), 0))
                bv = sch.op('vector', lambda e, c=c, u=u: e.tensor_copy(
                    evq[u % 2][:], ps[c][:]), inc='ve')
                bv_of[(o, c)] = bv
                sch.wait('tensor', 've', bv)
                if (u % 2) in av_prev:
                    sch.wait('tensor', 'ac', av_prev[u % 2])
                tv = None
                for j in range(4):
                    tv = sch.op('tensor', lambda e, j=j, u=u: e.transpose(
                        pstb[u % 2][:, j * 128:(j + 1) * 128],
                        evq[u % 2][:, j * 128:(j + 1) * 128], idb_s[:]), inc='pe')
                off = 0
                if o < 40:
                    sch.wait('scalar', 'pe', tv)
                    if (u % 2) in vv_prev:
                        sch.wait('scalar', 've', vv_prev[u % 2])
                    av = None
                    for j in range(4):
                        av = sch.op('scalar', lambda e, j=j, u=u: e.activation(
                            sqd[:, j * 128:(j + 1) * 128], pstb[u % 2][:, j * 128:(j + 1) * 128],
                            AF.Square, accum_out=ms[:, u % 2 * 8 + j:u % 2 * 8 + j + 1]), inc='ac')
                    av_prev[u % 2] = av
                    sch.wait('vector', 'ac', av)
                    m4 = ms[:, u % 2 * 8:u % 2 * 8 + 4]
                    r4 = ms[:, u % 2 * 8 + 4:u % 2 * 8 + 8]
                    v = sch.op('vector', lambda e, m4=m4: e.tensor_scalar(
                        m4, m4, 1.0 / 128.0, float(EPS1), ALU.mult, ALU.add), inc='ve')
                    sch.wait('vector', 've', v)
                    v = sch.op('vector', lambda e, m4=m4, r4=r4: e.reciprocal(r4, m4), inc='ve')
                    sch.wait('scalar', 've', v)
                    av = sch.op('scalar', lambda e, r4=r4: e.activation(r4, r4, AF.Sqrt), inc='ac')
                    sch.wait('vector', 'ac', av)
                    vv = None
                    for j in range(4):
                        vv = sch.op('vector', lambda e, j=j, u=u, r4=r4: e.tensor_scalar(
                            qn[u % 2][:, j * 128:(j + 1) * 128],
                            pstb[u % 2][:, j * 128:(j + 1) * 128],
                            r4[:, j:j + 1], None, ALU.mult), inc='ve')
                    vv_prev[u % 2] = vv
                else:
                    sch.wait('vector', 'pe', tv)
                    vv = sch.op('vector', lambda e, u=u: e.tensor_copy(
                        qn[u % 2][:], pstb[u % 2][:, 0:512]), inc='ve')
                sch.wait('sync', 've', vv)
                st_q[(o, c)] = sch.dma(
                    'sync', qkvt_o[o, c].rearrange("p j k -> p (j k)"), qn[u % 2][:],
                    inc='st%d' % (u % 2))

        tiles = [(cod_u, amx_u, wup_o, HSH_PAD, i) for i in range(NDT)] + \
                [(cod_d, amx_d, wdn_o, DM, i) for i in range(11)]
        dqv = {0: 0, 1: 0}
        for ti, (cod_dram, amx_dram, out_dram, width, dt_i) in enumerate(tiles):
            b = ti % 2
            if ti >= 2:
                sch.wait('gpsimd', 'dq%d' % b, dqv[b])
            lc = sch.dma('gpsimd', cod[b][:, 0:width],
                         cod_dram[dt_i * 128:(dt_i + 1) * 128, :], inc='ldc%d' % b)
            lc = sch.dma('gpsimd', amx[b][0:64, 0:width],
                         amx_dram[2 * dt_i:2 * dt_i + 1, :].broadcast_to([64, width]),
                         inc='ldc%d' % b)
            lc = sch.dma('gpsimd', amx[b][64:128, 0:width],
                         amx_dram[2 * dt_i + 1:2 * dt_i + 2, :].broadcast_to([64, width]),
                         inc='ldc%d' % b)
            sch.wait('vector', 'ldc%d' % b, lc)
            v = sch.op('vector', lambda e, b=b, width=width: e.tensor_scalar(
                dacc[b][:, 0:width], cod[b][:, 0:width], 0.0, NF4[0],
                ALU.is_equal, ALU.mult), inc='ve')
            for vi in range(1, 16):
                v2 = sch.op('vector', lambda e, b=b, vi=vi, width=width: e.tensor_scalar(
                    dtmp[:, 0:width], cod[b][:, 0:width], float(vi), NF4[vi],
                    ALU.is_equal, ALU.mult), inc='ve')
                sch.wait('vector', 've', v2)
                v = sch.op('vector', lambda e, b=b, width=width: e.tensor_add(
                    dacc[b][:, 0:width], dacc[b][:, 0:width], dtmp[:, 0:width]), inc='ve')
                sch.wait('vector', 've', v)
            v = sch.op('vector', lambda e, b=b, width=width: e.tensor_mul(
                dacc[b][:, 0:width], dacc[b][:, 0:width], amx[b][:, 0:width]), inc='ve')
            sch.wait('sync', 've', v)
            dqv[b] = sch.dma('sync', out_dram[dt_i * 128:(dt_i + 1) * 128, :],
                             dacc[b][:, 0:width], inc='dq%d' % b)

        sch.wait('sync', 'dq0', sch.val('dq0'))
        sch.wait('sync', 'dq1', sch.val('dq1'))
        sch.wait('sync', 'st0', sch.val('st0'))
        sch.wait('sync', 'st1', sch.val('st1'))
        sch.wait('sync', 'st', sch.val('st'))
        sch.run(block)
    return nc



SMAX = 16
NHT = HF // 128
SH_TAB = [[min(WC[h], SKV_A[h // 4]), min(WC[h], SKV_B[h // 4])] for h in range(H1)]


def bias_col(h, gi, s, j):
    return ((h * 2 + gi) * SMAX + s) * 4 + j


def build_l1b():
    nc = bass.Bass("TRN2", target_bir_lowering=False)
    x1T = nc.dram_tensor("x1T", [DM, TLOC], F32, kind="ExternalInput").ap()
    xnT = nc.dram_tensor("xnT", [DM, TLOC], BF16, kind="ExternalInput").ap()
    qf = nc.dram_tensor("qf", [DM, TLOC], BF16, kind="ExternalInput").ap()
    karr = nc.dram_tensor("karr", [HKV, 2, SMAX, 128, 512], BF16, kind="ExternalInput").ap()
    varr = nc.dram_tensor("varr", [HKV, 2, SMAX, 128, 512], BF16, kind="ExternalInput").ap()
    biast = nc.dram_tensor("biast", [128, H1 * 2 * SMAX * 4], F32, kind="ExternalInput").ap()
    rel5 = nc.dram_tensor("rel5", [5, 128, 512], F32, kind="ExternalInput").ap()
    wnq = nc.dram_tensor("wnq", [128, 1], F32, kind="ExternalInput").ap()
    wnk = nc.dram_tensor("wnk", [128, 1], F32, kind="ExternalInput").ap()
    idb = nc.dram_tensor("idb", [128, 128], BF16, kind="ExternalInput").ap()
    wo1 = nc.dram_tensor("wo1", [NDT, 128, DM], F32, kind="ExternalInput").ap()
    wup = nc.dram_tensor("wup", [NHT, 128, DM], BF16, kind="ExternalInput").ap()
    wdn = nc.dram_tensor("wdn", [NDT, 128, NHT * 128], BF16, kind="ExternalInput").ap()
    bup = nc.dram_tensor("bup", [128, NHT], F32, kind="ExternalInput").ap()
    bdn = nc.dram_tensor("bdn", [128, NDT], F32, kind="ExternalInput").ap()
    lbup = nc.dram_tensor("lbup", [16, HF], F32, kind="ExternalInput").ap()
    ladn = nc.dram_tensor("ladn", [128, NHT * 16], F32, kind="ExternalInput").ap()
    lbdn = nc.dram_tensor("lbdn", [16, DM], F32, kind="ExternalInput").ap()
    u1 = nc.dram_tensor("u1", [16, TLOC], BF16, kind="ExternalInput").ap()

    out_o = nc.dram_tensor("outF", [DM, TLOC], F32, kind="ExternalOutput").ap()
    o_scr = nc.dram_tensor("o_scr", [DM, TLOC], BF16, kind="Internal").ap()
    g1_scr = nc.dram_tensor("g1_scr", [DM, TLOC], BF16, kind="Internal").ap()

    with contextlib.ExitStack() as ctx:
        sch = Sch(nc, ctx)
        S = lambda name, shape, dt=F32: ctx.enter_context(nc.sbuf_tensor(name, shape, dt))
        wfold = S("wfold", [128, 4], F32)

        pss = [ctx.enter_context(nc.psum_tensor(f"pss{i}", [128, 512], F32)) for i in range(2)]
        pstb = [ctx.enter_context(nc.psum_tensor(f"pstb{i}", [128, 1024], BF16)) for i in range(2)]
        pso = ctx.enter_context(nc.psum_tensor("pso", [128, 512], F32))
        psu2 = ctx.enter_context(nc.psum_tensor("psu2", [16, 512], F32))
        block = ctx.enter_context(nc.Block())

        ld0 = sch.dma('gpsimd', wfold[:, 0:1], wnq[:], inc='ld')
        ld0 = sch.dma('gpsimd', wfold[:, 1:2], wnk[:], inc='ld')
        sch.wait('vector', 'ld', ld0)
        v = sch.op('vector', lambda e: e.tensor_mul(wfold[:, 2:3], wfold[:, 0:1], wfold[:, 1:2]), inc='ve')
        sch.wait('vector', 've', v)
        WF = sch.op('vector', lambda e: e.tensor_scalar_mul(
            wfold[:, 3:4], wfold[:, 2:3], 1.0 / math.sqrt(128.0)), inc='ve')

        with contextlib.ExitStack() as actx:
            S2 = lambda name, shape, dt: actx.enter_context(nc.sbuf_tensor(name, shape, dt))
            ksb = [S2(f"ksb{i}", [128, SMAX * 512], BF16) for i in range(2)]
            vsb = [S2(f"vsb{i}", [128, SMAX * 4, 128], BF16) for i in range(2)]
            panels = S2("panels", [128, SMAX * 4, 256], BF16)
            att = S2("att", [128, SMAX * 512], BF16)
            qh = [S2(f"qh{i}", [128, TLOC], BF16) for i in range(2)]
            qw = S2("qw", [128, TLOC], BF16)
            scrb = [S2(f"scrb{i}", [128, 512], F32) for i in range(2)]
            zc = S2("zc", [128, 24], F32)
            ob = [S2(f"ob{i}", [128, 512], BF16) for i in range(2)]
            idb_s = S2("idb_s", [128, 128], BF16)
            bia_s = S2("bia_s", [128, H1 * 2 * SMAX * 4], F32)
            rel_s = S2("rel_s", [128, 5, 512], F32)
            rels = S2("rels", [128, 5, 512], F32)

            ld0 = sch.dma('gpsimd', idb_s[:], idb[:], inc='ld2')
            ld0 = sch.dma('gpsimd', bia_s[:], biast[:], inc='ld2')
            ld0 = sch.dma('gpsimd', rel_s[:], rel5.transpose([1, 0, 2]), inc='ld2')
            sch.wait('tensor', 'ld2', ld0)
            sch.wait('scalar', 'ld2', ld0)
            sch.wait('vector', 'ld2', ld0)
            sch.wait('tensor', 've', WF)
            kv_done = 0
            kv_done_prev = 0
            qh_done = {0: 0, 1: 0}
            st_o = {0: 0, 1: 0}
            exp_prev = {0: 0, 1: 0}
            hq_idx = 0
            for kv in range(HKV):
                for gi in range(2):
                    it = kv * 2 + gi
                    bkv = it % 2
                    S_g = (SKV_A if gi == 0 else SKV_B)[kv]
                    if it >= 2:
                        sch.wait('gpsimd', 'pe', kv_done_prev)
                    lk = sch.dma('gpsimd',
                                 ksb[bkv][:, 0:S_g * 512]
                                 .rearrange("p (s w) -> p s w", s=S_g),
                                 karr[kv, gi, 0:S_g].transpose([1, 0, 2]), inc='ldk%d' % bkv)
                    lk = sch.dma('gpsimd',
                                 vsb[bkv][:, 0:S_g * 4, :]
                                 .rearrange("p (s kb) d -> p s (kb d)", s=S_g),
                                 varr[kv, gi, 0:S_g].transpose([1, 0, 2]), inc='ldk%d' % bkv)
                    sch.wait('tensor', 'ldk%d' % bkv, lk)
                    kv_done_prev = kv_done
                    for hq in range(4):
                        h = kv * 4 + hq
                        bq = hq_idx % 2
                        if hq_idx >= 2:
                            sch.wait('gpsimd', 've', qh_done[bq])
                        lq = sch.dma('gpsimd', qh[bq][:], qf[h * 128:(h + 1) * 128, :],
                                     inc='ldq%d' % bq)
                        S_h = SH_TAB[h][gi]
                        s0 = S_g - S_h
                        sch.wait('vector', 'ldq%d' % bq, lq)
                        sch.wait('vector', 'pe', kv_done)
                        sch.wait('vector', 've', WF)
                        qwv = sch.op('vector', lambda e, bq=bq: e.tensor_scalar(
                            qw[:], qh[bq][:], wfold[:, 3:4], None, ALU.mult), inc='ve')
                        for vv5 in range(5):
                            rv = sch.op('vector', lambda e, vv5=vv5, h=h: e.tensor_scalar(
                                rels[:, vv5, :], rel_s[:, vv5, :], float(SLOPES[h]),
                                None, ALU.mult), inc='ve')
                        sch.wait('tensor', 've', rv)
                        qh_done[bq] = rv
                        hq_idx += 1
                        pan_last = None
                        pv_half = 0
                        for j in range(4):
                            jj = j % 2
                            lhs = qw[:, gi * 512 + j * 128: gi * 512 + (j + 1) * 128]
                            exp_v = {}
                            for si in range(S_h):
                                s = s0 + si
                                b = si % 2
                                sch.wait('tensor', 'ac', exp_prev[b])
                                pv = sch.mm_group(
                                    [(lhs, ksb[bkv][:, s * 512:(s + 1) * 512])],
                                    pss[b][:], inc='pe')
                                variant = (1 + j) if s == S_g - 1 else 0
                                sch.wait('vector', 'pe', pv)
                                vv = sch.op('vector', lambda e, b=b, variant=variant:
                                            e.scalar_tensor_tensor(
                                                scrb[b][:], pss[b][:], 0.0,
                                                rels[:, variant, :],
                                                ALU.add, ALU.add), inc='ve')
                                sch.wait('scalar', 've', vv)
                                col = bias_col(h, gi, s, j)
                                av = sch.op('scalar', lambda e, b=b, si=si, col=col:
                                            e.activation(
                                                att[:, si * 512:(si + 1) * 512],
                                                scrb[b][:], AF.Exp,
                                                bias=bia_s[:, col:col + 1],
                                                accum_out=zc[:, si:si + 1]), inc='ac')
                                exp_v[si] = av
                                exp_prev[b] = av
                            sch.wait('vector', 'ac', exp_v[S_h - 1])
                            v = sch.op('vector', lambda e, S_h=S_h: e.reduce_sum(
                                zc[:, 16:17], zc[:, 0:S_h], axis=AX), inc='ve')
                            sch.wait('vector', 've', v)
                            v = sch.op('vector', lambda e: e.reciprocal(
                                zc[:, 17:18], zc[:, 16:17]), inc='ve')
                            sch.wait('vector', 've', v)
                            normv = sch.op('vector', lambda e, S_h=S_h: e.tensor_scalar(
                                att[:, 0:S_h * 512], att[:, 0:S_h * 512],
                                zc[:, 17:18], None, ALU.mult), inc='ve')
                            sch.wait('tensor', 've', normv)
                            pan_cp = {}
                            for kt in range(S_h * 4):
                                half = (kt // 4) % 2
                                if kt >= 8:
                                    sch.wait('tensor', 've', pan_cp[kt - 8])
                                tv = sch.op('tensor', lambda e, kt=kt, half=half: e.transpose(
                                    pstb[half][:, (kt % 4) * 128:(kt % 4) * 128 + 128],
                                    att[:, kt * 128:(kt + 1) * 128], idb_s[:]), inc='pe')
                                if kt % 4 == 3 or kt == S_h * 4 - 1:
                                    k0g = (kt // 4) * 4
                                    nk4 = kt - k0g + 1
                                    sch.wait('vector', 'pe', tv)
                                    if j >= 2:
                                        sch.wait('vector', 'pe', pv_half)
                                    pv4 = sch.op('vector', lambda e, k0g=k0g, nk4=nk4,
                                                 half=half, jj=jj, s0=s0: e.tensor_copy(
                                        panels[:, s0 * 4 + k0g:s0 * 4 + k0g + nk4,
                                               jj * 128:(jj + 1) * 128],
                                        pstb[half][:, 0:nk4 * 128]
                                        .rearrange("p (a b) -> p a b", a=nk4)), inc='ve')
                                    for z_ in range(k0g, kt + 1):
                                        pan_cp[z_] = pv4
                            pan_last = pan_cp[S_h * 4 - 1]
                            if j % 2 == 1:
                                sch.wait('tensor', 've', pan_last)
                                jh = j // 2
                                pairs = [(vsb[bkv][:, kt, :], panels[:, kt, :])
                                         for kt in range(s0 * 4, S_g * 4)]
                                pv_half = sch.mm_group(
                                    pairs, pso[:, jh * 256:(jh + 1) * 256], inc='pe')
                        pvv = pv_half
                        kv_done = pvv
                        bo_ = hq_idx % 2
                        sch.wait('vector', 'pe', pvv)
                        if st_o[bo_]:
                            sch.wait('vector', 'sto%d' % bo_, st_o[bo_])
                        ov = sch.op('vector', lambda e, bo_=bo_: e.tensor_copy(
                            ob[bo_][:], pso[:]), inc='ve')
                        sch.wait('sync', 've', ov)
                        st_o[bo_] = sch.dma(
                            'sync', o_scr[h * 128:(h + 1) * 128, gi * 512:(gi + 1) * 512],
                            ob[bo_][:], inc='sto%d' % bo_)
                        sch.wait('tensor', 've', ov)
            ATT_VE = sch.val('ve')
            ATT_AC = sch.val('ac')
            ATT_PE = sch.val('pe')
            sch.wait('sync', 'sto0', sch.val('sto0'))
            sch.wait('sync', 'sto1', sch.val('sto1'))

        for eng in ('gpsimd', 'sync'):
            sch.wait(eng, 've', ATT_VE)
            sch.wait(eng, 'ac', ATT_AC)
            sch.wait(eng, 'pe', ATT_PE)
            sch.wait(eng, 'sto0', sch.val('sto0'))
            sch.wait(eng, 'sto1', sch.val('sto1'))
        sch.wait('tensor', 've', ATT_VE)

        with contextlib.ExitStack() as fctx:
            S3 = lambda name, shape, dt: fctx.enter_context(nc.sbuf_tensor(name, shape, dt))
            x2 = S3("x2", [128, NHT, 512], BF16)
            u1s = S3("u1s", [16, TLOC], BF16)
            u2s = S3("u2s", [16, 512], BF16)
            lbup_s = S3("lbup_s", [16, HF], BF16)
            ladn_s = S3("ladn_s", [128, NHT * 16], BF16)
            lbdn_s = S3("lbdn_s", [16, DM], BF16)
            bup_s = S3("bup_s", [128, NHT], F32)
            bdn_s = S3("bdn_s", [128, NDT], F32)
            gvb = [S3(f"gvb{i}", [128, 512], BF16) for i in range(2)]

            lc = sch.dma('gpsimd', u1s[:], u1[:], inc='ldf')
            lc = sch.dma('gpsimd', lbup_s[:], lbup[:], inc='ldf')
            lc = sch.dma('gpsimd', ladn_s[:], ladn[:], inc='ldf')
            lc = sch.dma('gpsimd', lbdn_s[:], lbdn[:], inc='ldf')
            lc = sch.dma('gpsimd', bup_s[:], bup[:], inc='ldf')
            lc = sch.dma('gpsimd', bdn_s[:], bdn[:], inc='ldf')
            sch.wait('tensor', 'ldf', lc)
            sch.wait('scalar', 'ldf', lc)

            stg = {0: 0, 1: 0}
            nwu = {0: 0, 1: 0}
            nwd = {0: 0, 1: 0}
            pe_wu = {0: 0, 1: 0}
            pe_wd = {0: 0, 1: 0}
            pss_ac = {0: 0, 1: 0}
            u2v = 0
            FF_PE = FF_AC = 0
            for tc in range(2):
                with contextlib.ExitStack() as yctx:
                    S4 = lambda name, shape, dt: yctx.enter_context(nc.sbuf_tensor(name, shape, dt))
                    xn_sb = S4("xn_sb%d" % tc, [128, NDT, 512], BF16)
                    wu = [S4(f"wu{tc}{i}", [128, DM], BF16) for i in range(2)]
                    if tc > 0:
                        sch.wait('gpsimd', 'pe', FF_PE)
                        sch.wait('gpsimd', 'ac', FF_AC)
                    lx = sch.dma('gpsimd', xn_sb[:],
                                 xnT[:, tc * 512:(tc + 1) * 512]
                                 .rearrange("(nd p) t -> p nd t", p=128), inc='ldx')
                    sch.wait('tensor', 'ldx', 16 * (tc + 1))
                    for ht in range(NHT):
                        b = ht % 2
                        if ht >= 2:
                            sch.wait('gpsimd', 'pe', pe_wu[b])
                        nwu[b] += 1
                        sch.dma('gpsimd', wu[b][:], wup[ht], inc='ldwu%d' % b)
                        sch.wait('tensor', 'ldwu%d' % b, 16 * nwu[b])
                        pairs = [(wu[b][:, d * 128:(d + 1) * 128], xn_sb[:, d, :])
                                 for d in range(NDT)]
                        pairs.append((lbup_s[0:16, ht * 128:(ht + 1) * 128],
                                      u1s[0:16, tc * 512:(tc + 1) * 512]))
                        sch.wait('tensor', 'ac', pss_ac[b])
                        if ht == 0 and tc > 0:
                            sch.wait('tensor', 've', u2v)
                        pv = sch.mm_group(pairs, pss[b][:], inc='pe')
                        pe_wu[b] = pv
                        sch.wait('scalar', 'pe', pv)
                        av = sch.op('scalar', lambda e, ht=ht, b=b: e.activation(
                            x2[:, ht, :], pss[b][:], AF.Relu,
                            bias=bup_s[:, ht:ht + 1]), inc='ac')
                        pss_ac[b] = av
                        sch.wait('tensor', 'ac', av)
                        sch.op('tensor', lambda e, ht=ht: e.matmul(
                            psu2[:], ladn_s[:, ht * 16:(ht + 1) * 16], x2[:, ht, :],
                            start=(ht == 0), stop=(ht == NHT - 1),
                            skip_group_check=True),
                            inc=('pe' if ht == NHT - 1 else None))
                    U2PE = sch.val('pe')
                    sch.wait('vector', 'pe', U2PE)
                    u2v = sch.op('vector', lambda e: e.tensor_copy(u2s[:], psu2[:]), inc='ve')
                with contextlib.ExitStack() as yctx2:
                    S5 = lambda name, shape, dt: yctx2.enter_context(nc.sbuf_tensor(name, shape, dt))
                    wd = [S5(f"wd{tc}{i}", [128, NHT * 128], BF16) for i in range(2)]
                    sch.wait('gpsimd', 've', u2v)
                    for dt_i in range(NDT):
                        b = dt_i % 2
                        if dt_i >= 2:
                            sch.wait('gpsimd', 'pe', pe_wd[b])
                        nwd[b] += 1
                        sch.dma('gpsimd', wd[b][:], wdn[dt_i], inc='ldwd%d' % b)
                        sch.wait('tensor', 'ldwd%d' % b, 16 * nwd[b])
                        pairs = [(wd[b][:, ht * 128:(ht + 1) * 128], x2[:, ht, :])
                                 for ht in range(NHT)]
                        pairs.append((lbdn_s[0:16, dt_i * 128:(dt_i + 1) * 128], u2s[0:16, :]))
                        sch.wait('tensor', 'ac', pss_ac[b])
                        if dt_i < 2:
                            sch.wait('tensor', 've', u2v)
                        pv = sch.mm_group(pairs, pss[b][:], inc='pe')
                        pe_wd[b] = pv
                        sch.wait('scalar', 'pe', pv)
                        if stg[b]:
                            sch.wait('scalar', 'stg%d' % b, stg[b])
                        av = sch.op('scalar', lambda e, dt_i=dt_i, b=b: e.activation(
                            gvb[b][:], pss[b][:], AF.Silu,
                            bias=bdn_s[:, dt_i:dt_i + 1]), inc='ac')
                        pss_ac[b] = av
                        sch.wait('sync', 'ac', av)
                        stg[b] = sch.dma(
                            'sync', g1_scr[dt_i * 128:(dt_i + 1) * 128,
                                           tc * 512:(tc + 1) * 512],
                            gvb[b][:], inc='stg%d' % b)
                    FF_PE = sch.val('pe')
                    FF_AC = sch.val('ac')
            sch.wait('sync', 'stg0', sch.val('stg0'))
            sch.wait('sync', 'stg1', sch.val('stg1'))
            for eng in ('gpsimd', 'sync'):
                sch.wait(eng, 'pe', FF_PE)
                sch.wait(eng, 'ac', FF_AC)
                sch.wait(eng, 've', sch.val('ve'))
                sch.wait(eng, 'stg0', sch.val('stg0'))
                sch.wait(eng, 'stg1', sch.val('stg1'))
        sch.wait('tensor', 'ac', FF_AC)

        with contextlib.ExitStack() as pctx:
            S6 = lambda name, shape, dt: pctx.enter_context(nc.sbuf_tensor(name, shape, dt))
            o_sb = S6("o_sb", [128, NDT, 512], BF16)
            wos = [S6(f"wos{i}", [128, DM], BF16) for i in range(2)]
            x1t = [S6(f"x1t{i}", [128, 512], F32) for i in range(2)]
            g1t = [S6(f"g1t{i}", [128, 512], BF16) for i in range(2)]
            tmpc = [S6(f"tmpc{i}", [128, 512], F32) for i in range(2)]
            outb = [S6(f"outb{i}", [128, 512], F32) for i in range(2)]

            nwo = {0: 0, 1: 0}
            nxg = {0: 0, 1: 0}
            sto = {0: 0, 1: 0}
            cmb_v = {0: 0, 1: 0}
            pe_wo = {0: 0, 1: 0}
            for tc in range(2):
                if tc > 0:
                    sch.wait('gpsimd', 'pe', sch.val('pe'))
                lo = sch.dma('gpsimd', o_sb[:],
                             o_scr[:, tc * 512:(tc + 1) * 512]
                             .rearrange("(nd p) t -> p nd t", p=128), inc='ldo')
                sch.wait('tensor', 'ldo', 16 * (tc + 1))
                for dt_i in range(NDT):
                    u = tc * NDT + dt_i
                    b = u % 2
                    if u >= 2:
                        sch.wait('gpsimd', 'pe', pe_wo[b])
                        sch.wait('gpsimd', 've', cmb_v[b])
                    nwo[b] += 1
                    sch.dma('gpsimd', wos[b][:], wo1[dt_i], inc='ldwo%d' % b)
                    nxg[b] += 1
                    lg = sch.dma('gpsimd', x1t[b][:],
                                 x1T[dt_i * 128:(dt_i + 1) * 128, tc * 512:(tc + 1) * 512],
                                 inc='ldg%d' % b)
                    lg = sch.dma('gpsimd', g1t[b][:],
                                 g1_scr[dt_i * 128:(dt_i + 1) * 128, tc * 512:(tc + 1) * 512],
                                 inc='ldg%d' % b)
                    sch.wait('tensor', 'ldwo%d' % b, 16 * nwo[b])
                    pairs = [(wos[b][:, j * 128:(j + 1) * 128], o_sb[:, j, :])
                             for j in range(NDT)]
                    if u < 2:
                        sch.wait('tensor', 've', cmb_v[b])
                    else:
                        sch.wait('tensor', 've', cmb_v[b])
                    pv = sch.mm_group(pairs, pss[b][:], inc='pe')
                    pe_wo[b] = pv
                    sch.wait('vector', 'pe', pv)
                    sch.wait('vector', 'ldg%d' % b, lg)
                    if sto[b]:
                        sch.wait('vector', 'sto_f%d' % b, sto[b])
                    v = sch.op('vector', lambda e, b=b: e.tensor_mul(
                        tmpc[b][:], pss[b][:], g1t[b][:]), inc='ve')
                    sch.wait('vector', 've', v)
                    v = sch.op('vector', lambda e, b=b: e.tensor_add(
                        outb[b][:], tmpc[b][:], x1t[b][:]), inc='ve')
                    cmb_v[b] = v
                    sch.wait('sync', 've', v)
                    sto[b] = sch.dma(
                        'sync', out_o[dt_i * 128:(dt_i + 1) * 128, tc * 512:(tc + 1) * 512],
                        outb[b][:], inc='sto_f%d' % b)
            sch.wait('sync', 'sto_f0', sch.val('sto_f0'))
            sch.wait('sync', 'sto_f1', sch.val('sto_f1'))
        sch.run(block)
    return nc



def l1b_slot_gc(core):
    out = np.full((HKV, 2, SMAX), -1, dtype=np.int64)
    for kv in range(HKV):
        for gi, ch in enumerate((core, 15 - core)):
            S_g = (SKV_A if gi == 0 else SKV_B)[kv]
            for s in range(S_g):
                gc = ch - (S_g - 1) + s
                out[kv, gi, s] = gc if gc >= 0 else -1
    return out


def l1b_bias_table(core):
    tab = np.full((H1 * 2 * SMAX * 4,), -1e30, dtype=np.float32)
    gc = l1b_slot_gc(core)
    for h in range(H1):
        kv = h // 4
        for gi, ch in enumerate((core, 15 - core)):
            S_g = (SKV_A if gi == 0 else SKV_B)[kv]
            S_h = SH_TAB[h][gi]
            for s in range(S_g - S_h, S_g):
                g = gc[kv, gi, s]
                if g < 0:
                    continue
                for j in range(4):
                    q0 = ch * 512 + j * 128
                    k0 = g * 512
                    tab[bias_col(h, gi, s, j)] = SLOPES[h] * (k0 - q0) - B_SHIFT
    return np.ascontiguousarray(np.broadcast_to(tab[None, :], (128, tab.shape[0])).astype(np.float32))


def l1b_kv_arrays(core, K_fm, V_tm):
    karr = np.zeros((HKV, 2, SMAX, 128, 512), dtype=BF)
    varr = np.zeros((HKV, 2, SMAX, 128, 512), dtype=BF)
    gc = l1b_slot_gc(core)
    for kv in range(HKV):
        for gi in range(2):
            for s in range(SMAX):
                g = gc[kv, gi, s]
                if g < 0:
                    continue
                karr[kv, gi, s] = K_fm[kv * 128:(kv + 1) * 128, g * 512:(g + 1) * 512]
                vch = V_tm[g * 512:(g + 1) * 512, kv * 128:(kv + 1) * 128]
                varr[kv, gi, s] = vch.reshape(4, 128, 128).transpose(1, 0, 2).reshape(128, 512)
    return np.ascontiguousarray(karr), np.ascontiguousarray(varr)



_PROGS = {}


def _prog(name, builder):
    if name not in _PROGS:
        _PROGS[name] = builder()
    return _PROGS[name]


def _shard_rows(c):
    return np.r_[c * 512:(c + 1) * 512, (15 - c) * 512:(16 - c) * 512]


def kernel(**inputs):
    g = lambda k: np.asarray(inputs[k])
    x_full = np.ascontiguousarray(g('L0_attn_x').reshape(N, DM).astype(np.float32))
    shards = [_shard_rows(c) for c in range(NCORES)]
    idb = np.eye(128, dtype=BF)
    idf = np.eye(128, dtype=np.float32)

    nc = _prog('ln0', build_ln0)
    res = run_bass_kernel_spmd(
        nc, [{"x": np.ascontiguousarray(x_full[shards[c]])} for c in range(NCORES)],
        core_ids=list(range(NCORES)))
    xn0 = np.zeros((N, DM), dtype=BF)
    for c in range(NCORES):
        xn0[shards[c]] = res.results[c]["xn"]

    Wqk_T = g('L0_attn_Wqk_w').T
    hh = np.arange(8)[:, None]
    ee = np.arange(64)[None, :]
    qperm = (hh * 128 + 2 * ee).reshape(-1)
    kperm = (hh * 128 + 2 * ee + 1).reshape(-1)
    big_T = np.concatenate([Wqk_T[:, qperm], Wqk_T[:, kperm], g('L0_attn_Wv_w').T], axis=1)
    b_qk = g('L0_attn_Wqk_b')
    common = dict(
        wqkv=tile_w(np.ascontiguousarray(big_T)),
        bqkv=cols128(np.concatenate([b_qk[qperm], b_qk[kperm], g('L0_attn_Wv_b')])),
        wrout=np.ascontiguousarray(
            g('L0_ffn_W_router').T.reshape(32, 128, 8).transpose(1, 0, 2).reshape(128, 256)),
        wa=tile_w(np.ascontiguousarray(np.transpose(g('L0_ffn_A'), (2, 0, 1)).reshape(4096, 128))),
        wbase=tile_w(g('L0_ffn_W_base').T),
        bbase=cols128(g('L0_ffn_b_base')),
        bstk=np.ascontiguousarray(
            g('L0_ffn_B').transpose(0, 2, 1).reshape(128, 4096)
            .reshape(128, 32, 128).transpose(1, 0, 2)),
        sel8=np.zeros((8, 128), dtype=np.float32),
        idf=idf, idb=idb,
    )
    for ei in range(8):
        common["sel8"][ei, ei * 16:(ei + 1) * 16] = 1.0
    nc = _prog('l0a', build_l0a)
    res = run_bass_kernel_spmd(
        nc, [dict(xnT=np.ascontiguousarray(xn0[shards[c]].T), **common)
             for c in range(NCORES)], core_ids=list(range(NCORES)))
    kf = np.zeros((512, N), dtype=BF)
    vt = np.zeros((N, 512), dtype=BF)
    qf_c, g0t_c = [], []
    for c in range(NCORES):
        r = res.results[c]
        qkv = r["qkv"]
        kf[:, shards[c]] = qkv[512:1024]
        vt[shards[c], :] = qkv[1024:1536].T
        qf_c.append(np.ascontiguousarray(qkv[0:512]))
        g0t_c.append(np.ascontiguousarray(r["g0"].T))
    kf = np.ascontiguousarray(kf)
    vt = np.ascontiguousarray(vt)

    common = dict(
        kf=kf, vt=vt,
        wo=np.ascontiguousarray(g('L0_attn_Wo_w').T),
        bo=np.ascontiguousarray(g('L0_attn_Wo_b')[None, :]),
        ones1=np.ones((1, 128), dtype=np.float32),
        idb=idb,
    )
    nc = _prog('l0b', build_l0b)
    res = run_bass_kernel_spmd(
        nc, [dict(x=np.ascontiguousarray(x_full[shards[c]]), qf=qf_c[c],
                  g0t=g0t_c[c], **common) for c in range(NCORES)],
        core_ids=list(range(NCORES)))
    x1_c = [np.ascontiguousarray(res.results[c]["x1"]) for c in range(NCORES)]
    xn1_c = [np.ascontiguousarray(res.results[c]["xn1"]) for c in range(NCORES)]

    wq1 = np.concatenate([g('L1_attn_Wq').T, g('L1_attn_Wk').T, g('L1_attn_Wv').T], axis=1)
    cod_u_pad = np.zeros((DM, 8 * HSH_PAD), dtype=np.int32)
    cod_u_pad[:, :HF] = g('L1_ffn_w_up_codes').reshape(HF, DM).T
    amx_u_pad = np.zeros((64, 8 * HSH_PAD), dtype=np.float32)
    amx_u_pad[:, :HF] = g('L1_ffn_w_up_absmax').reshape(HF, 64).T
    cod_d_pad = np.zeros((8 * HSH_PAD, DM), dtype=np.int32)
    cod_d_pad[:HF] = g('L1_ffn_w_down_codes').reshape(DM, HF).T
    amx_d_pad = np.zeros((176, DM), dtype=np.float32)
    amx_d_pad[:172] = g('L1_ffn_w_down_absmax').reshape(DM, 172).T
    common = dict(
        wqkv=tile_w(np.ascontiguousarray(wq1)),
        idb=idb,
        la_up=np.ascontiguousarray(
            g('L1_ffn_w_up_lora_a').reshape(32, 128, 16).transpose(1, 0, 2).reshape(128, 512)),
    )
    nc = _prog('l1a', build_l1a)
    res = run_bass_kernel_spmd(
        nc, [dict(xnT=np.ascontiguousarray(xn1_c[c].T),
                  cod_u=np.ascontiguousarray(cod_u_pad[:, c * HSH_PAD:(c + 1) * HSH_PAD]),
                  amx_u=np.ascontiguousarray(amx_u_pad[:, c * HSH_PAD:(c + 1) * HSH_PAD]),
                  cod_d=np.ascontiguousarray(cod_d_pad[c * HSH_PAD:(c + 1) * HSH_PAD]),
                  amx_d=np.ascontiguousarray(amx_d_pad[c * 22:(c + 1) * 22]),
                  **common) for c in range(NCORES)], core_ids=list(range(NCORES)))
    K_fm = np.zeros((1024, N), dtype=BF)
    V_tm = np.zeros((N, 1024), dtype=BF)
    q_c, u1_c = [], []
    wup_full = np.zeros((DM, 8 * HSH_PAD), dtype=BF)
    wdn_full = np.zeros((8 * HSH_PAD, DM), dtype=BF)
    for c in range(NCORES):
        r = res.results[c]
        qkv_tm = r["qkvt"].transpose(1, 3, 2, 0, 4).reshape(1024, NQKV1 * 128)
        q_c.append(np.ascontiguousarray(qkv_tm[:, 0:4096].T))
        K_fm[:, shards[c]] = qkv_tm[:, 4096:5120].T
        V_tm[shards[c]] = qkv_tm[:, 5120:6144]
        wup_full[:, c * HSH_PAD:(c + 1) * HSH_PAD] = r["wup_sh"]
        wdn_full[c * HSH_PAD:(c + 1) * HSH_PAD] = r["wdn_sh"]
        u1_c.append(np.ascontiguousarray(r["u1"]))
    wup_til = tile_w(np.ascontiguousarray(wup_full[:, :HF]))
    wdn_til = tile_w(np.ascontiguousarray(wdn_full[:HF]))

    common = dict(
        rel5=_rel_consts(),
        wnq=np.ascontiguousarray(g('L1_attn_q_norm_w')[:, None].astype(np.float32)),
        wnk=np.ascontiguousarray(g('L1_attn_k_norm_w')[:, None].astype(np.float32)),
        idb=idb,
        wo1=tile_w(g('L1_attn_Wo').T),
        wup=wup_til, wdn=wdn_til,
        bup=np.ascontiguousarray(g('L1_ffn_b_up').reshape(NHT, 128).T.astype(np.float32)),
        bdn=cols128(g('L1_ffn_b_down')),
        lbup=np.ascontiguousarray(g('L1_ffn_w_up_lora_b').astype(np.float32)),
        ladn=np.ascontiguousarray(
            g('L1_ffn_w_down_lora_a').reshape(NHT, 128, 16).transpose(1, 0, 2)
            .reshape(128, NHT * 16).astype(np.float32)),
        lbdn=np.ascontiguousarray(g('L1_ffn_w_down_lora_b').astype(np.float32)),
    )
    in_maps = []
    for c in range(NCORES):
        karr, varr = l1b_kv_arrays(c, K_fm, V_tm)
        in_maps.append(dict(
            x1T=np.ascontiguousarray(x1_c[c].T),
            xnT=np.ascontiguousarray(xn1_c[c].T),
            qf=q_c[c],
            karr=karr, varr=varr,
            biast=l1b_bias_table(c),
            u1=u1_c[c],
            **common))
    nc = _prog('l1b', build_l1b)
    res = run_bass_kernel_spmd(nc, in_maps, core_ids=list(range(NCORES)))

    final = np.zeros((N, DM), dtype=np.float32)
    for c in range(NCORES):
        final[shards[c]] = res.results[c]["outF"].T
    return final.reshape(1, N, DM)
